# version 21
# speedup vs baseline: 1.9024x; 1.0010x over previous
import numpy as np
import ml_dtypes
import concourse.bass as bass
import concourse.mybir as mybir
from concourse.bass_utils import run_bass_kernel_spmd
from contextlib import ExitStack

F32 = mybir.dt.float32
BF16 = mybir.dt.bfloat16
AF = mybir.ActivationFunctionType
ALU = mybir.AluOpType
AX = mybir.AxisListType

MAGIC = 12582912.0
D, DI, DF = 1024, 2048, 4096
NDT, NIT, NFT = 8, 16, 32
ENGINES = ("tensor", "vector", "scalar", "gpsimd", "sync")


class Sem:
    __slots__ = ("h", "n", "name")

    def __init__(self, h, name):
        self.h, self.n, self.name = h, 0, name


class Builder:
    def __init__(self, nc, stack):
        self.nc, self.stack = nc, stack
        self.ops = {e: [] for e in ENGINES}

    def sem(self, name):
        return Sem(self.stack.enter_context(self.nc.semaphore(name)), name)

    def op(self, eng, fn, waits=(), inc=None, inc_n=1):
        if inc is not None:
            inc.n += inc_n
        self.ops[eng].append((tuple((s, c) for s, c in waits if c > 0), fn, inc, inc_n))
        return None if inc is None else inc.n

    def emit(self, block):
        for eng in ENGINES:
            dec = getattr(block, eng)

            def make(eng_ops):
                def body(e):
                    for waits, fn, inc, inc_n in eng_ops:
                        for s, cnt in waits:
                            e.wait_ge(s.h, cnt)
                        r = fn(e)
                        if inc is not None:
                            r.then_inc(inc.h, inc_n)
                return body

            dec(make(self.ops[eng]))


def chunks_of(total, size, base=0):
    out, o = [], 0
    while o < total:
        n = min(size, total - o)
        out.append((base + o, n))
        o += n
    return out


def OFF_VP(dt):
    return dt * 1036


def OFF_GP(ft):
    return 16576 + ft * 1024


def OFF_HN(i):
    return 32960 + i * 1036


def OFF_WB(j):
    return 41248 + j * 2048


OFF_B2S0 = 45344
ARB_N = 46368
ARF_N = 3648


class Prog:
    def __init__(self, T=2048, dump=None, ncores=1):
        assert T % 4 == 0
        self.ncores = ncores
        self.T, self.TH = T, T // 2
        self.FT = self.TH + 4
        self.dump = dump
        self.nc = bass.Bass("TRN2", target_bir_lowering=False, debug=False)

    def inp(self, name, shape, dt=F32):
        ap = self.nc.dram_tensor(name, shape, dt, kind="ExternalInput").ap()
        setattr(self, "i_" + name, ap)
        return ap

    def outp(self, name, shape, dt=F32):
        ap = self.nc.dram_tensor(name, shape, dt, kind="ExternalOutput").ap()
        return ap

    def build(self):
        nc = self.nc
        T, TH, FT, dump = self.T, self.TH, self.FT, self.dump

        for name, shape, dt in (
                ("xT", [D, T], F32), ("in_sw", [128, 32768], F32),
                ("pw_sw", [128, NIT * DI], F32), ("hb_wT", [DI, D], F32),
                ("fuse_sw", [128, 32768], F32),
                ("out_sw", [128, 16384], F32),
                ("ha_wT", [DI, D], F32), ("BmT", [DI, 16], F32),
                ("A_t", [128, 256], F32), ("dwa_t", [128, 64], F32),
                ("sym_t", [128, 144], F32),
                ("wgsh", [D // self.ncores, DF], F32),
                ("wush", [D // self.ncores, DF], F32),
                ("wdsh", [DF // self.ncores, D], F32),
                ("consts", [128, 8], F32), ("norm_w_c", [128, NDT], F32),
                ("in_b_c", [128, 4 * NDT], F32), ("dwa_b_cb", [128, NIT], BF16),
                ("sym_b_c", [128, NIT], F32), ("pw_b_cb", [128, NIT], BF16),
                ("hb_b_c", [128, NDT], F32), ("ha_b_c", [128, NDT], F32),
                ("fuse_b_c", [128, NIT], F32), ("out_b_r", [32, D], BF16),
                ("ones32c", [128, 32], BF16), ("ones32cf", [128, 32], F32),
                ("ones32rf", [32, 128], F32), ("onesNb", [32, 512], BF16),
                ("identb", [128, 128], BF16), ("identf", [128, 128], F32)):
            self.inp(name, shape, dt)

        adsp = "Shared" if self.ncores > 4 else "Local"
        self.sqg = nc.dram_tensor("sqg", [D, DF], BF16, kind="Internal",
                                  addr_space=adsp).ap()
        self.squ = nc.dram_tensor("squ", [D, DF], BF16, kind="Internal",
                                  addr_space=adsp).ap()
        self.sqd = nc.dram_tensor("sqd", [DF, D], BF16, kind="Internal",
                                  addr_space=adsp).ap()
        self.shg = nc.dram_tensor("shg", [D // self.ncores, DF], BF16,
                                  kind="Internal").ap()
        self.shu = nc.dram_tensor("shu", [D // self.ncores, DF], BF16,
                                  kind="Internal").ap()
        self.shd = nc.dram_tensor("shd", [DF // self.ncores, D], BF16,
                                  kind="Internal").ap()
        self.part_h = nc.dram_tensor("part_h", [3, 128], F32,
                                     kind="Internal").ap()
        self.red_h = nc.dram_tensor("red_h", [3, 128], F32,
                                    kind="Internal").ap()
        self.zhbm = nc.dram_tensor("zhbm", [T, DF], BF16, kind="Internal").ap()
        self.whp_hbm = nc.dram_tensor("whp_hbm", [128, NDT * NIT * 128], BF16, kind="Internal").ap()

        self.o_out = self.outp("o_out", [D, T])
        self.douts = {}
        if dump == "front":
            self.douts["d_hn"] = self.outp("d_hn", [D, FT], BF16)
            self.douts["d_gate"] = self.outp("d_gate", [DI, TH], BF16)
            self.douts["d_val"] = self.outp("d_val", [DI, FT + 8], BF16)
        elif dump == "pre":
            self.douts["d_whp"] = self.outp("d_whp", [DI, D], BF16)
            self.douts["d_g16"] = self.outp("d_g16", [32, DI], BF16)
            self.douts["d_ybr"] = self.outp("d_ybr", [32, DI], BF16)
            self.douts["d_hpb"] = self.outp("d_hpb", [128, NDT])
            self.douts["d_a4"] = self.outp("d_a4", [128, 1024], BF16)
            self.douts["d_scl"] = self.outp("d_scl", [1, 8])
            self.douts["d_sqg"] = self.outp("d_sqg", [D, DF], BF16)
        elif dump == "mid":
            self.douts["d_zb"] = self.outp("d_zb", [DI, TH], BF16)
            self.douts["d_a2"] = self.outp("d_a2", [32, TH], BF16)
            self.douts["d_b4"] = self.outp("d_b4", [D, TH], BF16)
            self.douts["d_yg"] = self.outp("d_yg", [DI, TH], BF16)
        elif dump == "attn":
            self.douts["d_rs"] = self.outp("d_rs", [D, T], BF16)
        elif dump == "ffn1":
            self.douts["d_rs"] = self.outp("d_rs", [D, T], BF16)
            self.douts["d_xq"] = self.outp("d_xq", [D, T], BF16)
            self.douts["d_m2"] = self.outp("d_m2", [128, 16])

        with ExitStack() as stack:
            E = stack.enter_context
            self.b = b = Builder(nc, stack)

            self.rs = [E(nc.sbuf_tensor(f"rs{i}", [128, T], F32)) for i in range(NDT)]
            self.xs = E(nc.sbuf_tensor("xs", [128, NDT * 4], F32))
            self.arb = E(nc.sbuf_tensor("arb", [128, ARB_N], BF16))
            self.arf = E(nc.sbuf_tensor("arf", [128, ARF_N], F32))
            self.rows = E(nc.sbuf_tensor("rows", [1, 1536], F32))
            self.sqs = E(nc.sbuf_tensor("sqs", [128, 512], BF16))
            self.cvk = E(nc.sbuf_tensor("cvk", [128, 512], BF16))
            self.sqx = E(nc.sbuf_tensor("sqx", [128, 512], BF16))
            self.cvp = E(nc.sbuf_tensor("cvp", [128, 512], BF16))
            self.a2sb = E(nc.sbuf_tensor("a2sb", [32, TH], BF16))
            self.g16p = E(nc.sbuf_tensor("g16p", [32, DI], BF16))
            self.m1b = E(nc.sbuf_tensor("m1b", [32, D], BF16))
            self.m1t = E(nc.sbuf_tensor("m1t", [128, NDT * 16], BF16))
            self.a4 = E(nc.sbuf_tensor("a4", [128, 64 * 16], BF16))
            self.atb = E(nc.sbuf_tensor("atb", [128, 256], BF16))
            self.bmt = E(nc.sbuf_tensor("bmt", [128, 256], BF16))
            self.ybr = E(nc.sbuf_tensor("ybr", [32, DI], BF16))
            self.ybcol = E(nc.sbuf_tensor("ybcol", [128, NIT], F32))
            self.arow = E(nc.sbuf_tensor("arow", [32, 16], BF16))
            self.abc = E(nc.sbuf_tensor("abc", [32, 1], BF16))
            self.habm = E(nc.sbuf_tensor("habm", [128, NDT], BF16))
            self.hpbc = E(nc.sbuf_tensor("hpbc", [128, NDT], F32))
            self.cst = E(nc.sbuf_tensor("cst", [128, 8], F32))
            self.nwc = E(nc.sbuf_tensor("nwc", [128, NDT], F32))
            self.ibc = E(nc.sbuf_tensor("ibc", [128, 4 * NDT], F32))
            self.dbc = E(nc.sbuf_tensor("dbc", [128, NIT], BF16))
            self.sbc = E(nc.sbuf_tensor("sbc", [128, NIT], F32))
            self.pbcb = E(nc.sbuf_tensor("pbcb", [128, NIT], BF16))
            self.hbbc = E(nc.sbuf_tensor("hbbc", [128, NDT], F32))
            self.habcb = E(nc.sbuf_tensor("habcb", [128, NDT], F32))
            self.fbc = E(nc.sbuf_tensor("fbc", [128, NIT], F32))
            self.obr = E(nc.sbuf_tensor("obr", [32, D], BF16))
            self.o32c = E(nc.sbuf_tensor("o32c", [128, 32], BF16))
            self.o32cf = E(nc.sbuf_tensor("o32cf", [128, 32], F32))
            self.o32rf = E(nc.sbuf_tensor("o32rf", [32, 128], F32))
            self.oNb = E(nc.sbuf_tensor("oNb", [32, 512], BF16))
            self.idb = E(nc.sbuf_tensor("idb", [128, 128], BF16))
            self.idf = E(nc.sbuf_tensor("idf", [128, 128], F32))
            self.symt = E(nc.sbuf_tensor("symt", [128, 144], F32))
            self.scl = E(nc.sbuf_tensor("scl", [1, 8], F32))
            self.swcol = E(nc.sbuf_tensor("swcol", [128, 6], F32))
            self.accw = E(nc.sbuf_tensor("accw", [128, 40], F32))
            self.m2run = E(nc.sbuf_tensor("m2run", [128, 16], F32))
            self.cq = E(nc.sbuf_tensor("cq", [128, 8], F32))
            self.ixcol = E(nc.sbuf_tensor("ixcol", [128, 16], F32))
            self.colg = E(nc.sbuf_tensor("colg", [128, 16], F32))
            self.colu = E(nc.sbuf_tensor("colu", [128, 16], F32))
            self.colD = E(nc.sbuf_tensor("colD", [128, 16], F32))

            self.pbk = [E(nc.psum_tensor(f"pbk{i}", [128, 512], F32))
                        for i in range(7)]
            self.ptb = E(nc.psum_tensor("ptb", [128, 1024], BF16))

            self.s_lwb = [b.sem("s_lwb0"), b.sem("s_lwb1")]
            self.s_lhn = b.sem("s_lhn")
            self.s_lrs = b.sem("s_lrs")
            self.s_lsm = b.sem("s_lsm")
            self.s_ldh = b.sem("s_ldh")
            self.s_lf = [b.sem("s_lf0"), b.sem("s_lf1")]
            self.s_pe = b.sem("s_pe")
            self.s_act = b.sem("s_act")
            self.s_dve = b.sem("s_dve")
            self.s_st = b.sem("s_st")
            self.s_sqb = [b.sem("s_sqb0"), b.sem("s_sqb1")]
            self.s_zst = [b.sem("s_zst0"), b.sem("s_zst1")]
            self.s_lqg = b.sem("s_lqg")
            self.s_lqu = b.sem("s_lqu")
            self.s_lsd = b.sem("s_lsd")
            self.s_whp = [b.sem("s_whp0"), b.sem("s_whp1")]
            self.s_pool = b.sem("s_pool")
            self.s_cc = b.sem("s_cc")
            self.s_ccd = b.sem("s_ccd")
            self.whpst_free = [(self.s_pe, 0), (self.s_pe, 0)]
            self.s_out = b.sem("s_out")
            self.bank = {i: (self.s_pe, 0) for i in range(7)}
            self.wbfree = [(self.s_pe, 0), (self.s_pe, 0)]
            self.lffree = [(self.s_dve, 0), (self.s_dve, 0)]
            self.ptb_free = (self.s_dve, 0)

            self.p0_loads()
            self.sqb_st = [(self.s_sqb[0], 0), (self.s_sqb[1], 0)]
            self.w_swc = 0
            self._prep_gen = self.prep_wq_gen()
            self.pre_small()
            self.pump(8)
            self.pre_whp()
            self.pump(6)
            self.pre_m1_g16_yb()
            self.pump(10)
            if dump == "pre":
                self.dump_pre()
            else:
                prev_inproj = None
                for h in range(2):
                    prev_inproj = self.half(h, prev_inproj)
                    if dump in ("front", "mid") and h == 0:
                        break
                if dump == "attn":
                    self.dump_attn()
                elif dump in ("front", "mid"):
                    pass
                else:
                    self.ffn(prev_inproj)
            self.finish()

            with nc.Block() as block:
                b.emit(block)
        return self

    def vop(self, fn, waits=()):
        return self.b.op("vector", fn, waits=waits, inc=self.s_dve)

    def aop(self, fn, waits=()):
        return self.b.op("scalar", fn, waits=waits, inc=self.s_act)

    def mm(self, fn, waits=(), fin=True):
        return self.b.op("tensor", fn, waits=waits,
                         inc=(self.s_pe if fin else None))

    def wbA(self, buf, off, n):
        o = OFF_WB(buf) + off
        return self.arb[:, o:o + n]

    def p0_loads(self):
        b, nc = self.b, self.nc
        T, TH = self.T, self.TH
        for i in range(NDT):
            b.op("sync", lambda e, i=i: e.dma_start(
                out=self.rs[i][:], in_=self.i_xT[128 * i:128 * (i + 1), :]),
                inc=self.s_lrs, inc_n=16)
        self.w_x = self.s_lrs.n
        for t_, src in ((self.cst, self.i_consts), (self.nwc, self.i_norm_w_c),
                        (self.ibc, self.i_in_b_c), (self.sbc, self.i_sym_b_c),
                        (self.hbbc, self.i_hb_b_c), (self.fbc, self.i_fuse_b_c),
                        (self.habcb, self.i_ha_b_c), (self.o32rf, self.i_ones32rf),
                        (self.o32cf, self.i_ones32cf),
                        (self.idf, self.i_identf), (self.symt, self.i_sym_t),
                        (self.dbc, self.i_dwa_b_cb),
                        (self.pbcb, self.i_pw_b_cb), (self.obr, self.i_out_b_r),
                        (self.o32c, self.i_ones32c), (self.oNb, self.i_onesNb),
                        (self.idb, self.i_identb)):
            b.op("sync", lambda e, t_=t_, src=src: e.dma_start(out=t_[:], in_=src),
                 inc=self.s_ldh, inc_n=16)
        b.op("sync", lambda e: e.dma_start(out=self.arf[:, 3584:3648],
                                           in_=self.i_dwa_t),
             inc=self.s_ldh, inc_n=16)
        self.w_ldh = self.s_ldh.n
        b.op("gpsimd", lambda e: e.dma_start(out=self.atb[:], in_=self.i_A_t),
             inc=self.s_lsm, inc_n=16)
        b.op("gpsimd", lambda e: e.dma_start(
            out=self.bmt[:], in_=self.i_BmT.rearrange("(kt p) s -> p kt s", p=128)),
            inc=self.s_lsm, inc_n=16)
        self.w_lsm = self.s_lsm.n

        for ft in range(NDT):
            self.w_xs = self.vop(lambda e, ft=ft: e.tensor_copy(
                self.xs[:, 4 * ft:4 * ft + 4], self.rs[ft][:, TH - 4:TH]),
                waits=[(self.s_lrs, self.w_x)])
        for t_ in (self.a2sb, self.g16p, self.m1b, self.ybr, self.arow, self.abc,
                   self.scl):
            self.vop(lambda e, t_=t_: e.memset(t_[:], 0.0))
        self.w_zpad = self.s_dve.n

    def pre_small(self):
        s_pe, s_dve = self.s_pe, self.s_dve
        for dt in range(NIT):
            for k in range(4):
                self.vop(lambda e, dt=dt, k=k: e.tensor_scalar(
                    out=self.a4[:, (dt * 4 + k) * 16:(dt * 4 + k) * 16 + 16],
                    in0=self.atb[:, dt * 16:dt * 16 + 16],
                    scalar1=self.arf[:, 3584 + dt * 4 + k:3584 + dt * 4 + k + 1],
                    scalar2=None, op0=ALU.mult),
                    waits=[(self.s_lsm, self.w_lsm), (self.s_ldh, self.w_ldh)])
        self.w_a4 = self.s_dve.n
        self.lffree[1] = (self.s_dve, self.w_a4)

        last = None
        for dt in range(NIT):
            last = self.mm(lambda e, dt=dt: e.matmul(
                self.pbk[0][0:16, 0:1], self.atb[:, dt * 16:dt * 16 + 16],
                self.dbc[:, dt:dt + 1], start=(dt == 0), stop=(dt == NIT - 1)),
                waits=([(self.s_lsm, self.w_lsm), (self.s_ldh, self.w_ldh),
                        self.bank[0]] if dt == 0 else []),
                fin=(dt == NIT - 1))
        self.w_ab = self.vop(lambda e: e.tensor_copy(
            self.abc[0:16, 0:1], self.pbk[0][0:16, 0:1]),
            waits=[(s_pe, last), (s_dve, self.w_zpad)])
        self.bank[0] = (s_dve, self.w_ab)
        w_abt = self.mm(lambda e: e.transpose(
            self.ptb[0:1, 0:32], self.abc[:, 0:1], self.idb[0:32, 0:32]),
            waits=[(s_dve, self.w_ab), self.ptb_free])
        self.w_abr = self.vop(lambda e: e.tensor_copy(
            self.arow[0:1, 0:16], self.ptb[0:1, 0:16]),
            waits=[(s_pe, w_abt), (s_dve, self.w_zpad)])
        self.ptb_free = (s_dve, self.w_abr)

    def pre_whp(self):
        b, arb = self.b, self.arb
        s_pe, s_act, s_dve = self.s_pe, self.s_act, self.s_dve
        hn_free = (s_dve, 0)
        for n in range(2):
            for i in range(NDT):
                b.op("gpsimd", lambda e, i=i, n=n: e.dma_start(
                    out=arb[:, OFF_HN(i):OFF_HN(i) + 1024],
                    in_=self.i_hb_wT.rearrange("(kt p) j -> p kt j", p=128)
                    [:, 2 * i:2 * i + 2, 512 * n:512 * (n + 1)]),
                    waits=[hn_free], inc=self.s_lhn, inc_n=16)
            w_hbn = self.s_lhn.n

            def hbn_ap(kt, j0, jn):
                o = OFF_HN(kt // 2) + (kt % 2) * 512
                return arb[:, o + j0:o + j0 + jn]

            for m in range(NIT):
                buf = m % 2
                w_pw = b.op("gpsimd", lambda e, m=m, buf=buf: e.dma_start(
                    out=self.wbA(buf, 0, 2048),
                    in_=self.i_pw_sw[:, 2048 * m:2048 * (m + 1)]),
                    waits=[self.wbfree[buf]], inc=self.s_lwb[buf], inc_n=16)
                last = None
                wbr = self.arb[:, OFF_WB(buf):OFF_WB(buf) + 2048].rearrange(
                    "p (c kt) -> p kt c", kt=NIT)
                for kt in range(NIT):
                    last = self.mm(lambda e, kt=kt, buf=buf, n=n, wbr=wbr:
                                   e.matmul(
                                       self.pbk[n][:, :],
                                       wbr[:, kt, :],
                                       hbn_ap(kt, 0, 512),
                                       start=(kt == 0), stop=(kt == NIT - 1)),
                                   waits=([(self.s_lwb[buf], w_pw),
                                           (self.s_lhn, w_hbn),
                                           self.bank[n]] if kt == 0 else []),
                                   fin=(kt == NIT - 1))
                self.wbfree[buf] = (s_pe, last)
                stg = OFF_GP(0) + 512 * (m % 2)
                wev = self.aop(lambda e, m=m, n=n, stg=stg: e.activation(
                    out=arb[:, stg:stg + 512],
                    in_=self.pbk[n][:, :], func=AF.Identity),
                    waits=[(s_pe, last), self.whpst_free[m % 2]])
                self.bank[n] = (s_act, wev)
                b.op("sync", lambda e, m=m, n=n, stg=stg: e.dma_start(
                    out=self.whp_hbm.rearrange(
                        "p (jt kt js) -> p jt kt js", jt=NDT, kt=NIT)
                    [:, 4 * n:4 * n + 4, m, :],
                    in_=arb[:, stg:stg + 512].rearrange(
                        "p (jt js) -> p jt js", jt=4)),
                    waits=[(s_act, wev)], inc=self.s_whp[m % 2], inc_n=16)
                self.whpst_free[m % 2] = (self.s_whp[m % 2], self.s_whp[m % 2].n)
            for jj in range(4):
                jt = n * 4 + jj
                for kt in range(NIT):
                    last = self.mm(lambda e, jj=jj, jt=jt, kt=kt, hbn_ap=hbn_ap:
                                   e.matmul(
                                       self.pbk[2][:, jt:jt + 1],
                                       hbn_ap(kt, 128 * jj, 128),
                                       self.pbcb[:, kt:kt + 1],
                                       start=(kt == 0), stop=(kt == NIT - 1)),
                                   waits=([self.bank[2]]
                                          if jt == 0 and kt == 0 else []),
                                   fin=(kt == NIT - 1))
            hn_free = (s_pe, s_pe.n)
        self.w_hpb = self.vop(lambda e: e.tensor_tensor(
            out=self.hpbc[:], in0=self.hbbc[:], in1=self.pbk[2][:, 0:8],
            op=ALU.add),
            waits=[(s_pe, s_pe.n), (self.s_ldh, self.w_ldh)])
        self.bank[2] = (s_dve, self.w_hpb)
        self.whp_ready = self.s_act.n
        self.hn_free = hn_free

    def pre_m1_g16_yb(self):
        b, arb = self.b, self.arb
        s_pe, s_act, s_dve = self.s_pe, self.s_act, self.s_dve
        for n in range(2):
            m1last = None
            for kt in range(NIT):
                buf = kt % 2
                w_ha = b.op("gpsimd", lambda e, kt=kt, n=n, buf=buf: e.dma_start(
                    out=self.wbA(buf, 0, 512),
                    in_=self.i_ha_wT.rearrange("(kt p) j -> p kt j", p=128)
                    [:, kt, 512 * n:512 * (n + 1)]),
                    waits=[self.wbfree[buf]], inc=self.s_lwb[buf], inc_n=16)
                m1last = self.mm(lambda e, kt=kt, buf=buf: e.matmul(
                    self.pbk[3][0:16, 0:512],
                    self.bmt[:, kt * 16:kt * 16 + 16],
                    self.wbA(buf, 0, 512),
                    start=(kt == 0), stop=(kt == NIT - 1)),
                    waits=[(self.s_lwb[buf], w_ha), (self.s_lsm, self.w_lsm)]
                    + ([self.bank[3]] if kt == 0 else []), fin=True)
                self.wbfree[buf] = (s_pe, m1last)
            w_m1 = self.vop(lambda e, n=n: e.tensor_copy(
                self.m1b[0:16, 512 * n:512 * (n + 1)], self.pbk[3][0:16, 0:512]),
                waits=[(s_pe, m1last), (s_dve, self.w_zpad)])
            self.bank[3] = (s_dve, w_m1)
        w_m1all = self.s_dve.n
        w_m1t = 0
        for jt in range(NDT):
            wtp = self.mm(lambda e, jt=jt: e.transpose(
                self.ptb[:, 32 * (jt % 4):32 * (jt % 4) + 16],
                self.m1b[0:16, 128 * jt:128 * (jt + 1)], self.idb[0:16, 0:16]),
                waits=[(s_dve, w_m1all), (s_dve, w_m1t), self.ptb_free])
            w_m1t = self.vop(lambda e, jt=jt: e.tensor_copy(
                self.m1t[:, 16 * jt:16 * (jt + 1)],
                self.ptb[:, 32 * (jt % 4):32 * (jt % 4) + 16]),
                waits=[(s_pe, wtp)])
        self.w_m1t = w_m1t
        self.ptb_free = (s_dve, w_m1t)

        w_habm = self.vop(lambda e: e.tensor_copy(self.habm[:], self.habcb[:]),
                          waits=[(self.s_ldh, self.w_ldh)])

        yb_first = True
        last = None
        for ch in range(8):
            buf = ch % 2
            w_fa = b.op("gpsimd", lambda e, ch=ch, buf=buf: e.dma_start(
                out=self.wbA(buf, 0, 2048),
                in_=self.i_fuse_sw[:, 4096 * ch:4096 * ch + 2048]),
                waits=[self.wbfree[buf]], inc=self.s_lwb[buf], inc_n=16)
            g16last = None
            for jt in range(NDT):
                g16last = self.mm(lambda e, jt=jt, buf=buf: e.matmul(
                    self.pbk[5][0:16, 0:256],
                    self.m1t[:, 16 * jt:16 * (jt + 1)],
                    self.wbA(buf, 256 * jt, 256),
                    start=(jt == 0), stop=(jt == NDT - 1)),
                    waits=([(self.s_lwb[buf], w_fa), (s_dve, w_m1t),
                            self.bank[5]] if jt == 0 else []),
                    fin=(jt == NDT - 1))
            w_g16 = self.vop(lambda e, ch=ch: e.tensor_copy(
                self.g16p[0:16, 256 * ch:256 * (ch + 1)],
                self.pbk[5][0:16, 0:256]),
                waits=[(s_pe, g16last), (s_dve, self.w_zpad)])
            self.bank[5] = (s_dve, w_g16)
            for f2 in range(2):
                ft = ch * 2 + f2
                for jt in range(NDT):
                    w_ = []
                    if jt == 0:
                        w_ = [(s_dve, w_habm)]
                        if yb_first:
                            w_.append(self.bank[6])
                            yb_first = False
                    last = self.mm(lambda e, f2=f2, jt=jt, buf=buf, ft=ft:
                                   e.matmul(
                                       self.pbk[6][:, ft:ft + 1],
                                       self.wbA(buf, 256 * jt + 128 * f2, 128),
                                       self.habm[:, jt:jt + 1],
                                       start=(jt == 0), stop=(jt == NDT - 1)),
                                   waits=w_, fin=(jt == NDT - 1))
            self.wbfree[buf] = (s_pe, last)
        w_ybc = self.vop(lambda e: e.tensor_tensor(
            out=self.ybcol[:], in0=self.fbc[:], in1=self.pbk[6][:, 0:16],
            op=ALU.add),
            waits=[(s_pe, self.s_pe.n), (self.s_ldh, self.w_ldh)])
        self.bank[6] = (s_dve, w_ybc)
        w_ybr = 0
        for ft in range(NIT):
            wtp = self.mm(lambda e, ft=ft: e.transpose(
                self.pbk[0][0:1, 0:128], self.ybcol[:, ft:ft + 1], self.idf[:]),
                waits=[(s_dve, w_ybc), self.bank[0], (s_dve, w_ybr)])
            w_ybr = self.vop(lambda e, ft=ft: e.tensor_copy(
                self.ybr[0:1, 128 * ft:128 * (ft + 1)], self.pbk[0][0:1, 0:128]),
                waits=[(s_pe, wtp), (s_dve, self.w_zpad)])
        self.bank[0] = (s_dve, w_ybr)
        self.w_ybr = w_ybr

    def prep_wq_gen(self):
        b, arb, arf = self.b, self.arb, self.arf
        s_pe, s_act, s_dve = self.s_pe, self.s_act, self.s_dve
        sqb_off = OFF_VP(13)
        sqb_st = [(self.s_sqb[0], 0), (self.s_sqb[1], 0)]
        tq_free = 0
        NCS = self.ncores
        grp = [list(range(NCS))]
        specs = ((self.i_wgsh, self.shg, self.sqg, D, DF),
                 (self.i_wush, self.shu, self.squ, D, DF),
                 (self.i_wdsh, self.shd, self.sqd, DF, D))
        wchunks = []
        for wsrc, sdst, gout, KD, OD in specs:
            ccw = min(1024, OD)
            wchunks.append([(kt, cc) for kt in range(KD // NCS // 128)
                            for cc in range(OD // ccw)])
        nch = len(wchunks[0])
        packed = 3 * nch <= 32
        w_red = {}
        w_pd = 0
        for wi, (wsrc, sdst, gout, KD, OD) in enumerate(specs):
            ccw = min(1024, OD)
            for ci, (kt, cc) in enumerate(wchunks[wi]):
                buf = ci % 2
                foff = 1024 * buf
                col = wi * nch + ci if packed else ci
                w_wc = b.op("sync" if buf == 0 else "scalar",
                            lambda e, wsrc=wsrc, kt=kt, cc=cc, foff=foff,
                            ccw=ccw: e.dma_start(
                                out=arf[:, foff:foff + ccw],
                                in_=wsrc[128 * kt:128 * (kt + 1),
                                         ccw * cc:ccw * (cc + 1)]),
                            waits=[self.lffree[buf]], inc=self.s_lf[buf],
                            inc_n=16)
                w_ab1 = self.aop(lambda e, foff=foff, col=col, ccw=ccw:
                                 e.activation(
                                     out=arf[:, foff:foff + ccw],
                                     in_=arf[:, foff:foff + ccw],
                                     func=AF.Abs,
                                     accum_out=self.accw[:, col:col + 1]),
                                 waits=[(self.s_lf[buf], w_wc)]
                                 + ([] if packed or ci > 0 or wi == 0
                                    else [(s_dve, w_red[wi - 1])]))
                self.lffree[buf] = (s_act, w_ab1)
                last_ab = w_ab1
                yield
            c0 = wi * nch if packed else 0
            w_red[wi] = self.vop(
                lambda e, c0=c0, n=nch, wi=wi: e.tensor_reduce(
                    self.accw[:, 32 + wi:33 + wi],
                    self.accw[:, c0:c0 + n], AX.X, ALU.add),
                waits=[(s_act, last_ab)])
            w_pd = b.op("gpsimd", lambda e, wi=wi: e.dma_start(
                out=self.part_h[wi:wi + 1, :],
                in_=self.accw[:, 32 + wi:33 + wi]),
                waits=[(s_dve, w_red[wi])], inc=self.s_ccd, inc_n=16)
            yield
        w_ar = b.op("gpsimd", lambda e, grp=grp: e.collective_compute(
            "AllReduce", ALU.add, replica_groups=grp,
            ins=[self.part_h[0:3, :]], outs=[self.red_h[0:3, :]]),
            waits=[(self.s_ccd, w_pd)], inc=self.s_cc)
        w_rl = 0
        for wi in range(3):
            w_rl = b.op("gpsimd", lambda e, wi=wi: e.dma_start(
                out=self.accw[:, 35 + wi:36 + wi],
                in_=self.red_h[wi:wi + 1, :]),
                waits=([(self.s_cc, w_ar)] if wi == 0 else []),
                inc=self.s_ccd, inc_n=16)
        yield
        w_sw = {}
        for wi, (wsrc, sdst, gout, KD, OD) in enumerate(specs):
            w_smm = self.mm(lambda e, wi=wi: e.matmul(
                self.pbk[1][0:32, 0:1], self.o32cf[:],
                self.accw[:, 35 + wi:36 + wi],
                start=True, stop=True),
                waits=[(self.s_ccd, w_rl), self.bank[1],
                       (self.s_ldh, self.w_ldh)])
            w_s1 = self.vop(lambda e, wi=wi, KD=KD, OD=OD: e.tensor_scalar(
                out=self.scl[0:1, 3 + wi:4 + wi], in0=self.pbk[1][0:1, 0:1],
                scalar1=1.0 / (KD * OD), scalar2=1e-5, op0=ALU.mult,
                op1=ALU.max),
                waits=[(s_pe, w_smm)])
            self.bank[1] = (s_dve, w_s1)
            w_s2 = self.vop(lambda e, wi=wi: e.reciprocal(
                self.scl[0:1, wi:wi + 1], self.scl[0:1, 3 + wi:4 + wi]),
                waits=[(s_dve, w_s1)])
            w_bc = self.mm(lambda e, wi=wi: e.matmul(
                self.pbk[1][:, 0:1], self.o32rf[0:1, :],
                self.scl[0:1, wi:wi + 1],
                start=True, stop=True),
                waits=[(s_dve, w_s2), self.bank[1]])
            w_bc2 = self.mm(lambda e, wi=wi: e.matmul(
                self.pbk[1][:, 1:2], self.o32rf[0:1, :],
                self.scl[0:1, 3 + wi:4 + wi],
                start=True, stop=True),
                waits=[(s_dve, w_s2)])
            w_swc = self.vop(lambda e, wi=wi: e.tensor_copy(
                self.swcol[:, wi:wi + 1], self.pbk[1][:, 0:1]),
                waits=[(s_pe, w_bc)])
            w_swc = self.vop(lambda e, wi=wi: e.tensor_copy(
                self.swcol[:, 3 + wi:4 + wi], self.pbk[1][:, 1:2]),
                waits=[(s_pe, w_bc2)])
            self.bank[1] = (s_dve, w_swc)
            w_sw[wi] = w_swc
            self.w_swc = w_swc
            yield
        for wi, (wsrc, sdst, gout, KD, OD) in enumerate(specs):
            ccw = min(1024, OD)
            for ci, (kt, cc) in enumerate(wchunks[wi]):
                buf = ci % 2
                foff = 1024 * buf
                w_qc = b.op("sync" if buf == 0 else "scalar",
                            lambda e, wsrc=wsrc, kt=kt, cc=cc, foff=foff,
                            ccw=ccw: e.dma_start(
                                out=arf[:, foff:foff + ccw],
                                in_=wsrc[128 * kt:128 * (kt + 1),
                                         ccw * cc:ccw * (cc + 1)]),
                            waits=[self.lffree[buf]], inc=self.s_lf[buf],
                            inc_n=16)
                w_q1 = self.vop(lambda e, foff=foff, wi=wi, ccw=ccw:
                                e.tensor_scalar(
                                    out=arf[:, 2048:2048 + ccw],
                                    in0=arf[:, foff:foff + ccw],
                                    scalar1=self.swcol[:, wi:wi + 1],
                                    scalar2=1.0, op0=ALU.mult, op1=ALU.min),
                                waits=[(self.s_lf[buf], w_qc),
                                       (s_dve, w_sw[wi]), (s_dve, tq_free)])
                w_q2 = self.vop(lambda e, ccw=ccw: e.tensor_scalar(
                    out=arf[:, 2048:2048 + ccw], in0=arf[:, 2048:2048 + ccw],
                    scalar1=-1.0, scalar2=MAGIC, op0=ALU.max, op1=ALU.add),
                    waits=[(s_dve, w_q1)])
                sb_off = sqb_off + 1024 * buf
                w_q3 = self.vop(lambda e, sb_off=sb_off, ccw=ccw:
                                e.tensor_scalar(
                                    out=arb[:, sb_off:sb_off + ccw],
                                    in0=arf[:, 2048:2048 + ccw],
                                    scalar1=-MAGIC, scalar2=None, op0=ALU.add),
                                waits=[(s_dve, w_q2), sqb_st[buf]])
                tq_free = w_q3
                self.lffree[buf] = (s_dve, w_q3)
                b.op("sync" if buf == 0 else "scalar",
                     lambda e, sdst=sdst, kt=kt, cc=cc, sb_off=sb_off,
                     ccw=ccw: e.dma_start(
                         out=sdst[128 * kt:128 * (kt + 1),
                                  ccw * cc:ccw * (cc + 1)],
                         in_=arb[:, sb_off:sb_off + ccw]),
                     waits=[(s_dve, w_q3)], inc=self.s_sqb[buf], inc_n=16)
                sqb_st[buf] = (self.s_sqb[buf], self.s_sqb[buf].n)
                self.sqb_st = sqb_st
                yield
        for wi, (wsrc, sdst, gout, KD, OD) in enumerate(specs):
            b.op("gpsimd", lambda e, sdst=sdst, gout=gout, grp=grp:
                 e.collective_compute(
                     "AllGather", ALU.bypass, replica_groups=grp,
                     ins=[sdst], outs=[gout]),
                 waits=list(sqb_st), inc=self.s_cc)
            yield
        self.sqb_st = sqb_st
        self.cc_done = (self.s_cc, self.s_cc.n)
        while True:
            yield

    def pump(self, n=1):
        for _ in range(n):
            if self._prep_gen is not None:
                try:
                    next(self._prep_gen)
                except StopIteration:
                    self._prep_gen = None

    def half(self, h, prev):
        b, arb = self.b, self.arb
        s_pe, s_act, s_dve = self.s_pe, self.s_act, self.s_dve
        T, TH, FT = self.T, self.TH, self.FT
        fr0 = h * (TH - 4)
        own0 = h * TH
        f_chunks = chunks_of(FT, 512)
        own_chunks = chunks_of(TH, 512)
        if prev is None:
            prev = dict(hn_region=[self.hn_free], vp_region=list(self.sqb_st),
                        gp_region=list(self.whpst_free), rs_ok=[],
                        b2_region=list(self.sqb_st), a2_readers=[])

        def vpA(dt, c0, n):
            o = OFF_VP(dt)
            return arb[:, o + c0:o + c0 + n]

        def b2A(dt, c0, n):
            o = OFF_B2S0 if dt == 0 else OFF_VP(dt - 1)
            return arb[:, o + c0:o + c0 + n]

        def gpA(ft, c0, n):
            o = OFF_GP(ft)
            return arb[:, o + c0:o + c0 + n]

        def hnA(kt, c0, n):
            o = OFF_HN(kt)
            return arb[:, o + c0:o + c0 + n]

        b4A = hnA

        def ygA(ft, c0, n):
            o = OFF_VP(ft)
            return arb[:, o + c0:o + c0 + n]

        rms_sq_free = prev.get("rms_sq_free", (s_pe, 0))
        w_hn_chunk = {}
        for (c0, cn) in f_chunks:
            self.pump(3)
            last = None
            for ft in range(NDT):
                pieces = []
                if h == 1 and c0 == 0:
                    pieces = [(0, 4, self.xs[:, 4 * ft:4 * ft + 4]),
                              (4, cn - 4, self.rs[ft][:, TH:TH + cn - 4])]
                else:
                    pieces = [(0, cn, self.rs[ft][:, fr0 + c0:fr0 + c0 + cn])]
                wsq = None
                for (p0, pn, src) in pieces:
                    wsq = self.aop(lambda e, p0=p0, pn=pn, src=src: e.activation(
                        out=self.sqs[:, p0:p0 + pn], in_=src, func=AF.Square),
                        waits=[(self.s_lrs, self.w_x), rms_sq_free,
                               (s_dve, self.w_xs)] + prev["rs_ok"])
                last = self.mm(lambda e, ft=ft, cn=cn: e.matmul(
                    self.pbk[6][0:32, 0:cn], self.o32c[:], self.sqs[:, 0:cn],
                    start=(ft == 0), stop=(ft == NDT - 1)),
                    waits=[(s_act, wsq)] + ([self.bank[6]] if ft == 0 else []),
                    fin=True)
                rms_sq_free = (s_pe, last)
            wt = self.vop(lambda e, cn=cn: e.tensor_scalar(
                out=self.rows[0:1, 0:cn], in0=self.pbk[6][0:1, 0:cn],
                scalar1=1.0 / D, scalar2=None, op0=ALU.mult),
                waits=[(s_pe, last)])
            self.bank[6] = (s_dve, wt)
            wl = self.aop(lambda e, cn=cn: e.activation(
                out=self.rows[0:1, 0:cn], in_=self.rows[0:1, 0:cn],
                func=AF.Ln, bias=self.cst[0:1, 0:1]),
                waits=[(s_dve, wt)])
            we = self.aop(lambda e, cn=cn: e.activation(
                out=self.rows[0:1, 0:cn], in_=self.rows[0:1, 0:cn],
                func=AF.Exp, scale=-0.5),
                waits=[(s_act, wl)])
            wb_ = self.mm(lambda e, cn=cn: e.matmul(
                self.pbk[5][:, 0:cn], self.o32rf[0:1, :], self.rows[0:1, 0:cn],
                start=True, stop=True),
                waits=[(s_act, we), self.bank[5]])
            wap = 0
            for ft in range(NDT):
                if h == 1 and c0 == 0:
                    pieces = [(0, 4, self.xs[:, 4 * ft:4 * ft + 4]),
                              (4, cn - 4, self.rs[ft][:, TH:TH + cn - 4])]
                else:
                    pieces = [(0, cn, self.rs[ft][:, fr0 + c0:fr0 + c0 + cn])]
                for (p0, pn, src) in pieces:
                    wa = self.aop(lambda e, ft=ft, c0=c0, p0=p0, pn=pn, src=src:
                                  e.activation(
                                      out=hnA(ft, c0 + p0, pn), in_=src,
                                      func=AF.Identity,
                                      scale=self.nwc[:, ft:ft + 1]),
                                  waits=[(s_pe, wb_)] + prev["hn_region"])
                    wap = self.vop(lambda e, ft=ft, c0=c0, p0=p0, pn=pn:
                                   e.tensor_tensor(
                                       out=hnA(ft, c0 + p0, pn),
                                       in0=hnA(ft, c0 + p0, pn),
                                       in1=self.pbk[5][:, p0:p0 + pn],
                                       op=ALU.mult),
                                   waits=[(s_act, wa)])
            self.bank[5] = (s_dve, wap)
            w_hn_chunk[c0] = self.s_dve.n
        w_hn_done = self.s_dve.n

        w_edge = 0
        early_dts = [dt for dt in range(NIT) if not (h == 0 and dt in (13, 14))]
        for dt in early_dts:
            b.op("vector", lambda e, dt=dt: e.memset(vpA(dt, 0, 4), 0.0),
                 waits=prev["vp_region"], inc=s_dve)
            w_edge = self.vop(lambda e, dt=dt: e.memset(vpA(dt, 4 + FT, 4), 0.0),
                              waits=prev["vp_region"])
        w_gate_done = 0
        w_val_done = 0
        w_val = {}
        for oc in range(16):
            self.pump(14)
            if h == 0 and oc == 14:
                self.pump(1 << 30)
                self.prep_done = [(self.s_dve, self.s_dve.n),
                                  (self.s_act, self.s_act.n)]
            buf = oc % 2
            wld = b.op("gpsimd", lambda e, oc=oc, buf=buf: e.dma_start(
                out=self.wbA(buf, 0, 2048),
                in_=self.i_in_sw[:, 2048 * oc:2048 * (oc + 1)]),
                waits=[self.wbfree[buf]], inc=self.s_lwb[buf], inc_n=16)
            mm_last = None
            for ot in range(2):
                o_global = oc * 2 + ot
                pbank = self.pbk[o_global % 4]
                for (c0, cn) in f_chunks:
                    for kt in range(NDT):
                        w_ = []
                        if kt == 0:
                            w_ = [(self.s_lwb[buf], wld),
                                  (s_dve, w_hn_chunk[c0]),
                                  self.bank[o_global % 4]]
                        mm_last = self.mm(lambda e, buf=buf, kt=kt, ot=ot, c0=c0,
                                          cn=cn, pbank=pbank: e.matmul(
                                              pbank[:, 0:cn],
                                              self.arb[:, OFF_WB(buf) + kt * 256 +
                                                       ot * 128:OFF_WB(buf) +
                                                       kt * 256 + (ot + 1) * 128],
                                              hnA(kt, c0, cn),
                                              start=(kt == 0), stop=(kt == NDT - 1)),
                                          waits=w_, fin=(kt == NDT - 1))
                    if o_global < NIT:
                        gt = o_global
                        lo = max(c0, 4 * h)
                        hi = min(c0 + cn, 4 * h + TH)
                        if hi > lo:
                            wev = self.aop(lambda e, gt=gt, c0=c0, lo=lo, hi=hi,
                                           pbank=pbank, h=h: e.activation(
                                               out=gpA(gt, lo - 4 * h, hi - lo),
                                               in_=pbank[:, lo - c0:hi - c0],
                                               func=AF.Sigmoid,
                                               bias=self.ibc[:, gt:gt + 1]),
                                           waits=[(s_pe, mm_last)]
                                           + prev["gp_region"])
                            w_gate_done = wev
                        else:
                            wev = None
                    else:
                        dt = o_global - NIT
                        xw = list(self.sqb_st) if (h == 0 and dt in (13, 14)) else []
                        wev = self.aop(lambda e, dt=dt, c0=c0, cn=cn, pbank=pbank:
                                       e.activation(
                                           out=vpA(dt, 4 + c0, cn),
                                           in_=pbank[:, 0:cn],
                                           func=AF.Identity,
                                           bias=self.ibc[:, NIT + dt:NIT + dt + 1]),
                                       waits=[(s_pe, mm_last),
                                              (s_dve, w_edge)] + xw)
                        w_val_done = wev
                        w_val[dt] = wev
                    self.bank[o_global % 4] = ((s_act, wev) if wev is not None
                                               else (s_pe, mm_last))
            self.wbfree[buf] = (s_pe, mm_last)
        if h == 0:
            for dt in (13, 14):
                b.op("vector", lambda e, dt=dt: e.memset(vpA(dt, 0, 4), 0.0),
                     waits=list(self.sqb_st) + prev["vp_region"], inc=s_dve)
                w_edge = self.vop(lambda e, dt=dt: e.memset(vpA(dt, 4 + FT, 4), 0.0),
                                  waits=list(self.sqb_st) + prev["vp_region"])
        w_inproj_done = self.s_pe.n

        if self.dump == "front" and h == 0:
            d = self.douts
            wall = [(s_act, self.s_act.n), (s_dve, self.s_dve.n)]
            for ft in range(NDT):
                b.op("sync", lambda e, ft=ft: e.dma_start(
                    out=d["d_hn"][128 * ft:128 * (ft + 1), :], in_=hnA(ft, 0, FT)),
                    waits=wall, inc=self.s_st, inc_n=16)
            for dt in range(NIT):
                b.op("sync", lambda e, dt=dt: e.dma_start(
                    out=d["d_gate"][128 * dt:128 * (dt + 1), :], in_=gpA(dt, 0, TH)),
                    waits=wall, inc=self.s_st, inc_n=16)
                b.op("sync", lambda e, dt=dt: e.dma_start(
                    out=d["d_val"][128 * dt:128 * (dt + 1), :],
                    in_=vpA(dt, 0, FT + 8)),
                    waits=wall, inc=self.s_st, inc_n=16)
            return {}

        a2bank = {0: self.pbk[6], 1: self.pbk[0]}
        a2bankid = {0: 6, 1: 0}
        accid = {0: 4, 1: 5}
        self.dgi = getattr(self, "dgi", 0)
        self.dslot_free = getattr(self, "dslot_free",
                                  {j: (s_pe, 0) for j in range(8)})
        w_amm = {}
        w_tap = {}
        w_zb_ci = {}
        w_a2_ci = {}
        w_b4_ci = {}
        w_yg_ci = {}
        bid = {0: 1, 1: 2}
        fid = {0: 3, 1: 4}
        oid = {0: 1, 1: 2}
        w_fuse_mm = 0
        w_yg_done = 0
        w_rs_done = 0
        w_gate = w_gate_done

        for ci, (l0, ln) in enumerate(own_chunks):
            for dt in range(NIT):
                self.pump(1)
                bid_c = 4 + (dt % 2)
                acc = self.pbk[bid_c]
                last_cm = None
                for k in range(9):
                    j = self.dgi % 8
                    self.dgi += 1
                    slot = self.cvk if j < 4 else self.cvp
                    off = 128 * (j % 4)
                    wb = self.vop(lambda e, dt=dt, k=k, slot=slot, off=off:
                                  e.tensor_scalar(
                                      out=slot[:, off:off + 128],
                                      in0=self.idb[:],
                                      scalar1=self.symt[:, dt * 9 + k:
                                                        dt * 9 + k + 1],
                                      scalar2=None, op0=ALU.mult),
                                  waits=[self.dslot_free[j],
                                         (self.s_ldh, self.w_ldh)])
                    w_ = [(s_dve, wb)]
                    if k == 0:
                        w_ += [(s_act, w_val[dt]), (s_dve, w_edge),
                               self.bank[bid_c]]
                    last_cm = self.mm(lambda e, dt=dt, k=k, l0=l0, ln=ln, h=h,
                                      slot=slot, off=off, acc=acc: e.matmul(
                                          acc[:, 0:ln],
                                          slot[:, off:off + 128],
                                          vpA(dt, l0 + 4 * h + k, ln),
                                          start=(k == 0), stop=(k == 8)),
                                      waits=w_, fin=True)
                    self.dslot_free[j] = (s_pe, last_cm)
                wev = self.aop(lambda e, dt=dt, l0=l0, ln=ln, acc=acc:
                               e.activation(
                                   out=b2A(dt, l0, ln), in_=acc[:, 0:ln],
                                   func=AF.Silu,
                                   bias=self.sbc[:, dt:dt + 1]),
                               waits=[(s_pe, last_cm),
                                      (s_pe, w_amm.get((dt - 1, ci), 0))]
                               + prev["b2_region"])
                self.bank[bid_c] = (s_act, wev)
                for k in range(4):
                    w_ = []
                    if k == 0:
                        w_ = [(s_act, w_val[dt])]
                    if dt == 0 and k == 0:
                        w_ += [self.bank[a2bankid[ci % 2]], (s_dve, self.w_a4),
                               (s_dve, w_edge)]
                    r = self.mm(lambda e, dt=dt, k=k, l0=l0, ln=ln, h=h, ci=ci:
                                e.matmul(
                                    a2bank[ci % 2][0:16, 0:ln],
                                    self.a4[:, (dt * 4 + k) * 16:
                                            (dt * 4 + k) * 16 + 16],
                                    vpA(dt, l0 + 4 * h + k + 3, ln),
                                    start=(dt == 0 and k == 0), stop=False),
                                waits=w_, fin=(k == 3))
                    if k == 3:
                        w_amm[(dt, ci)] = r
            wbias = self.mm(lambda e, l0=l0, ln=ln, ci=ci: e.matmul(
                a2bank[ci % 2][0:16, 0:ln], self.arow[0:1, 0:16],
                self.oNb[0:1, 0:ln], start=False, stop=True),
                waits=[(s_dve, self.w_abr), (self.s_ldh, self.w_ldh)])
            w_a2 = self.vop(lambda e, l0=l0, ln=ln, ci=ci: e.tensor_copy(
                self.a2sb[0:16, l0:l0 + ln], a2bank[ci % 2][0:16, 0:ln]),
                waits=[(s_pe, wbias), (s_dve, self.w_zpad)]
                + prev["a2_readers"])
            self.bank[a2bankid[ci % 2]] = (s_dve, w_a2)
            w_a2_ci[ci] = w_a2
            w_zb_ci[ci] = self.s_act.n

        for jt in range(NDT):
            buf = jt % 2
            wldw = b.op("gpsimd", lambda e, jt=jt, buf=buf: e.dma_start(
                out=self.wbA(buf, 0, 2048),
                in_=self.whp_hbm[:, 2048 * jt:2048 * (jt + 1)]),
                waits=[self.wbfree[buf],
                       (self.s_whp[0], self.s_whp[0].n),
                       (self.s_whp[1], self.s_whp[1].n)],
                inc=self.s_lwb[buf], inc_n=16)
            last = None
            for ci, (l0, ln) in enumerate(own_chunks):
                pbank = self.pbk[bid[ci % 2]]
                for kt in range(NIT):
                    w_ = []
                    if kt == 0:
                        w_ = [(s_act, w_zb_ci[ci]), self.bank[bid[ci % 2]],
                              (self.s_lwb[buf], wldw)]
                    last = self.mm(lambda e, jt=jt, kt=kt, l0=l0, ln=ln, buf=buf,
                                   pbank=pbank: e.matmul(
                                       pbank[:, 0:ln],
                                       arb[:, OFF_WB(buf) + 128 * kt:
                                           OFF_WB(buf) + 128 * (kt + 1)],
                                       b2A(kt, l0, ln),
                                       start=(kt == 0), stop=(kt == NIT - 1)),
                                   waits=w_, fin=(kt == NIT - 1))
                wev = self.aop(lambda e, jt=jt, l0=l0, ln=ln, pbank=pbank:
                               e.activation(
                                   out=b4A(jt, l0, ln), in_=pbank[:, 0:ln],
                                   func=AF.Identity,
                                   bias=self.hpbc[:, jt:jt + 1]),
                               waits=[(s_pe, last), (s_pe, w_inproj_done),
                                      (s_dve, w_hn_done)])
                self.bank[bid[ci % 2]] = (s_act, wev)
            self.wbfree[buf] = (s_pe, last)
        w_b4_done = self.s_act.n

        for oc in range(8):
            buf = oc % 2
            wld = b.op("gpsimd", lambda e, oc=oc, buf=buf: e.dma_start(
                out=self.wbA(buf, 0, 2048),
                in_=self.i_fuse_sw[:, 4096 * oc + 2048:4096 * (oc + 1)]),
                waits=[self.wbfree[buf]], inc=self.s_lwb[buf], inc_n=16)
            for f2 in range(2):
                ft = oc * 2 + f2
                for ci, (l0, ln) in enumerate(own_chunks):
                    pbank = self.pbk[fid[ci % 2]]
                    last = None
                    for jt in range(NDT):
                        w_ = []
                        if jt == 0:
                            w_ = [(self.s_lwb[buf], wld), (s_act, w_b4_done),
                                  self.bank[fid[ci % 2]]]
                        last = self.mm(lambda e, jt=jt, f2=f2, l0=l0, ln=ln,
                                       buf=buf, pbank=pbank: e.matmul(
                                           pbank[:, 0:ln],
                                           arb[:, OFF_WB(buf) + 256 * jt +
                                               128 * f2:OFF_WB(buf) + 256 * jt +
                                               128 * (f2 + 1)],
                                           b4A(jt, l0, ln),
                                           start=(jt == 0), stop=False),
                                       waits=w_, fin=False)
                    last = self.mm(lambda e, ft=ft, l0=l0, ln=ln, pbank=pbank:
                                   e.matmul(
                                       pbank[:, 0:ln],
                                       self.g16p[:, 128 * ft:128 * (ft + 1)],
                                       self.a2sb[:, l0:l0 + ln],
                                       start=False, stop=False),
                                   waits=[(s_dve, w_a2_ci[ci])], fin=False)
                    last = self.mm(lambda e, ft=ft, l0=l0, ln=ln, pbank=pbank:
                                   e.matmul(
                                       pbank[:, 0:ln],
                                       self.ybr[0:1, 128 * ft:128 * (ft + 1)],
                                       self.oNb[0:1, 0:ln],
                                       start=False, stop=True),
                                   waits=[(s_dve, self.w_ybr)], fin=True)
                    w_fuse_mm = last
                    wyg = self.vop(lambda e, ft=ft, l0=l0, ln=ln, pbank=pbank:
                                   e.tensor_tensor(
                                       out=ygA(ft, l0, ln),
                                       in0=gpA(ft, l0, ln),
                                       in1=pbank[:, 0:ln], op=ALU.mult),
                                   waits=[(s_pe, last), (s_act, w_gate)])
                    self.bank[fid[ci % 2]] = (s_dve, wyg)
                    w_yg_done = wyg
            self.wbfree[buf] = (s_pe, w_fuse_mm)

        for et in range(NDT):
            buf = et % 2
            wld = b.op("gpsimd", lambda e, et=et, buf=buf: e.dma_start(
                out=self.wbA(buf, 0, 2048),
                in_=self.i_out_sw[:, 2048 * et:2048 * (et + 1)]),
                waits=[self.wbfree[buf]], inc=self.s_lwb[buf], inc_n=16)
            for ci, (l0, ln) in enumerate(own_chunks):
                pbank = self.pbk[oid[ci % 2]]
                last = None
                for kt in range(NIT):
                    w_ = []
                    if kt == 0:
                        w_ = [(self.s_lwb[buf], wld), (s_dve, w_yg_done),
                              self.bank[oid[ci % 2]]]
                    last = self.mm(lambda e, kt=kt, l0=l0, ln=ln, buf=buf,
                                   pbank=pbank: e.matmul(
                                       pbank[:, 0:ln],
                                       arb[:, OFF_WB(buf) + 128 * kt:
                                           OFF_WB(buf) + 128 * (kt + 1)],
                                       ygA(kt, l0, ln),
                                       start=(kt == 0), stop=False),
                                   waits=w_, fin=False)
                last = self.mm(lambda e, et=et, l0=l0, ln=ln, pbank=pbank:
                               e.matmul(
                                   pbank[:, 0:ln],
                                   self.obr[0:1, 128 * et:128 * (et + 1)],
                                   self.oNb[0:1, 0:ln],
                                   start=False, stop=True),
                               waits=[(self.s_ldh, self.w_ldh)], fin=True)
                w_rs = self.vop(lambda e, et=et, l0=l0, ln=ln, pbank=pbank:
                                e.scalar_tensor_tensor(
                                    out=self.rs[et][:, own0 + l0:own0 + l0 + ln],
                                    in0=self.rs[et][:, own0 + l0:own0 + l0 + ln],
                                    scalar=2.0, op0=ALU.mult,
                                    in1=pbank[:, 0:ln], op1=ALU.add),
                                waits=[(s_pe, last), (s_dve, self.w_xs),
                                       (s_pe, rms_sq_free[1])])
                self.bank[oid[ci % 2]] = (s_dve, w_rs)
                w_rs_done = w_rs
            self.wbfree[buf] = (s_pe, self.s_pe.n)
        w_outproj_mm = self.s_pe.n

        if self.dump == "mid" and h == 0:
            d = self.douts
            wall = [(s_act, self.s_act.n), (s_dve, self.s_dve.n)]
            for dt in range(NIT):
                b.op("sync", lambda e, dt=dt: e.dma_start(
                    out=d["d_zb"][128 * dt:128 * (dt + 1), :], in_=b2A(dt, 0, TH)),
                    waits=wall, inc=self.s_st, inc_n=16)
                b.op("sync", lambda e, dt=dt: e.dma_start(
                    out=d["d_yg"][128 * dt:128 * (dt + 1), :], in_=ygA(dt, 0, TH)),
                    waits=wall, inc=self.s_st, inc_n=16)
            b.op("sync", lambda e: e.dma_start(out=d["d_a2"], in_=self.a2sb[:]),
                 waits=wall, inc=self.s_st, inc_n=16)
            for jt in range(NDT):
                b.op("sync", lambda e, jt=jt: e.dma_start(
                    out=d["d_b4"][128 * jt:128 * (jt + 1), :], in_=b4A(jt, 0, TH)),
                    waits=wall, inc=self.s_st, inc_n=16)
            return {}

        w_ln = self.layernorm_chunks(own0, own_chunks,
                                     wait_rs=(s_dve, w_rs_done))

        return dict(
            hn_region=[(s_pe, w_fuse_mm)],
            vp_region=[(s_pe, w_outproj_mm)],
            gp_region=[(s_dve, w_yg_done)],
            b2_region=[(s_pe, self.s_pe.n)],
            a2_readers=[(s_pe, w_fuse_mm)],
            rs_ok=[(s_dve, w_ln)],
            rms_sq_free=rms_sq_free,
        )

    def layernorm_chunks(self, own0, own_chunks, wait_rs):
        s_pe, s_act, s_dve = self.s_pe, self.s_act, self.s_dve
        w_last = 0
        for (l0, ln) in own_chunks:
            lastm = None
            lasts = None
            for ft in range(NDT):
                wcp = self.aop(lambda e, ft=ft, l0=l0, ln=ln: e.activation(
                    out=self.sqx[:, 0:ln],
                    in_=self.rs[ft][:, own0 + l0:own0 + l0 + ln],
                    func=AF.Identity),
                    waits=[wait_rs, (s_pe, lastm or 0)])
                lastm = self.mm(lambda e, ft=ft, l0=l0, ln=ln: e.matmul(
                    self.pbk[5][0:32, 0:ln], self.o32c[:], self.sqx[:, 0:ln],
                    start=(ft == 0), stop=(ft == NDT - 1)),
                    waits=[(s_act, wcp)] + ([self.bank[5]] if ft == 0 else []),
                    fin=True)
                wsq = self.aop(lambda e, ft=ft, l0=l0, ln=ln: e.activation(
                    out=self.sqs[:, 0:ln],
                    in_=self.rs[ft][:, own0 + l0:own0 + l0 + ln],
                    func=AF.Square),
                    waits=[wait_rs, (s_pe, lasts or 0)])
                lasts = self.mm(lambda e, ft=ft, ln=ln: e.matmul(
                    self.pbk[6][0:32, 0:ln], self.o32c[:], self.sqs[:, 0:ln],
                    start=(ft == 0), stop=(ft == NDT - 1)),
                    waits=[(s_act, wsq)] + ([self.bank[6]] if ft == 0 else []),
                    fin=True)
            wm = self.vop(lambda e, ln=ln: e.tensor_scalar(
                out=self.rows[0:1, 0:ln], in0=self.pbk[5][0:1, 0:ln],
                scalar1=1.0 / D, scalar2=None, op0=ALU.mult),
                waits=[(s_pe, lastm)])
            self.bank[5] = (s_dve, wm)
            wq = self.vop(lambda e, ln=ln: e.tensor_scalar(
                out=self.rows[0:1, 512:512 + ln], in0=self.pbk[6][0:1, 0:ln],
                scalar1=1.0 / D, scalar2=None, op0=ALU.mult),
                waits=[(s_pe, lasts)])
            self.bank[6] = (s_dve, wq)
            wv = self.vop(lambda e, ln=ln: e.tensor_tensor(
                out=self.rows[0:1, 1024:1024 + ln], in0=self.rows[0:1, 0:ln],
                in1=self.rows[0:1, 0:ln], op=ALU.mult),
                waits=[(s_dve, wm)])
            wv = self.vop(lambda e, ln=ln: e.tensor_tensor(
                out=self.rows[0:1, 512:512 + ln], in0=self.rows[0:1, 512:512 + ln],
                in1=self.rows[0:1, 1024:1024 + ln], op=ALU.subtract),
                waits=[(s_dve, wv), (s_dve, wq)])
            wl = self.aop(lambda e, ln=ln: e.activation(
                out=self.rows[0:1, 512:512 + ln], in_=self.rows[0:1, 512:512 + ln],
                func=AF.Ln, bias=self.cst[0:1, 1:2]),
                waits=[(s_dve, wv)])
            we = self.aop(lambda e, ln=ln: e.activation(
                out=self.rows[0:1, 512:512 + ln], in_=self.rows[0:1, 512:512 + ln],
                func=AF.Exp, scale=-0.5),
                waits=[(s_act, wl)])
            wbm = self.mm(lambda e, ln=ln: e.matmul(
                self.pbk[3][:, 0:ln], self.o32rf[0:1, :], self.rows[0:1, 0:ln],
                start=True, stop=True),
                waits=[(s_dve, wm), self.bank[3]])
            wbr = self.mm(lambda e, ln=ln: e.matmul(
                self.pbk[4][:, 0:ln], self.o32rf[0:1, :],
                self.rows[0:1, 512:512 + ln],
                start=True, stop=True),
                waits=[(s_act, we), self.bank[4]])
            for ft in range(NDT):
                w1 = self.vop(lambda e, ft=ft, l0=l0, ln=ln: e.tensor_tensor(
                    out=self.rs[ft][:, own0 + l0:own0 + l0 + ln],
                    in0=self.rs[ft][:, own0 + l0:own0 + l0 + ln],
                    in1=self.pbk[3][:, 0:ln], op=ALU.subtract),
                    waits=[(s_pe, wbm), (s_pe, lasts)])
                w_last = self.vop(lambda e, ft=ft, l0=l0, ln=ln: e.tensor_tensor(
                    out=self.rs[ft][:, own0 + l0:own0 + l0 + ln],
                    in0=self.rs[ft][:, own0 + l0:own0 + l0 + ln],
                    in1=self.pbk[4][:, 0:ln], op=ALU.mult),
                    waits=[(s_pe, wbr), (s_dve, w1)])
            self.bank[3] = (s_dve, w_last)
            self.bank[4] = (s_dve, w_last)
        return w_last

    def ffn(self, prev):
        b, arb, arf = self.b, self.arb, self.arf
        self.gs_free = 0
        self.cq_free = 0
        self.tqf_free = 0
        self.ztm_free = 0
        self.dtm_free = 0
        s_pe, s_act, s_dve = self.s_pe, self.s_act, self.s_dve
        T = self.T
        NTT = T // 128
        XQ0 = 16576
        WQG, WQU = 32960, 32960 + 4096
        ZQ0 = 0
        ZTM = OFF_WB(0)
        SD0 = 16576

        def xqA(ft, c0, n):
            o = XQ0 + ft * T
            return arb[:, o + c0:o + c0 + n]

        w_xq_done = 0
        w_q2p = 0
        w_xq_tt = {}
        w_cg_tt = {}
        for tt in range(NTT):
            wtp4 = wtp5 = None
            for ft in range(NDT):
                bk = self.pbk[4] if ft < 4 else self.pbk[5]
                wtp = self.mm(lambda e, tt=tt, ft=ft, bk=bk: e.transpose(
                    bk[:, 128 * (ft % 4):128 * (ft % 4 + 1)],
                    self.rs[ft][:, 128 * tt:128 * (tt + 1)], self.idf[:]),
                    waits=([self.bank[4], self.bank[5]] if ft == 0 else [])
                    + (prev["rs_ok"] if ft == 0 else []),
                    fin=(ft in (3, NDT - 1)))
                if ft == 3:
                    wtp4 = wtp
            wtp5 = wtp
            w_m = self.vop(lambda e: e.tensor_reduce(
                self.cq[:, 0:1], self.pbk[4][:, :], AX.X, ALU.max,
                apply_absolute_value=True),
                waits=[(s_pe, wtp4)])
            w_m2_ = self.vop(lambda e: e.tensor_reduce(
                self.cq[:, 7:8], self.pbk[5][:, :], AX.X, ALU.max,
                apply_absolute_value=True),
                waits=[(s_pe, wtp5)])
            w_m = self.vop(lambda e: e.tensor_tensor(
                out=self.cq[:, 0:1], in0=self.cq[:, 0:1], in1=self.cq[:, 7:8],
                op=ALU.max), waits=[(s_dve, w_m), (s_dve, w_m2_)])
            w_c = self.vop(lambda e: e.tensor_scalar(
                out=self.cq[:, 1:2], in0=self.cq[:, 0:1], scalar1=1e-5,
                scalar2=None, op0=ALU.max), waits=[(s_dve, w_m)])
            w_r = self.vop(lambda e: e.reciprocal(self.cq[:, 2:3], self.cq[:, 1:2]),
                           waits=[(s_dve, w_c)])
            w_sx = self.vop(lambda e: e.tensor_scalar(
                out=self.cq[:, 3:4], in0=self.cq[:, 2:3], scalar1=127.0,
                scalar2=None, op0=ALU.mult), waits=[(s_dve, w_r)])
            w_ix = self.vop(lambda e, tt=tt: e.tensor_scalar(
                out=self.ixcol[:, tt:tt + 1], in0=self.cq[:, 1:2],
                scalar1=1.0 / 127.0, scalar2=None, op0=ALU.mult),
                waits=[(s_dve, w_c)])
            w_q1a = self.vop(lambda e: e.tensor_scalar(
                out=arf[:, 2048:2560], in0=self.pbk[4][:, :],
                scalar1=self.cq[:, 3:4], scalar2=MAGIC,
                op0=ALU.mult, op1=ALU.add),
                waits=[(s_dve, w_sx), (s_dve, w_q2p)])
            w_q1 = self.vop(lambda e: e.tensor_scalar(
                out=arf[:, 2560:3072], in0=self.pbk[5][:, :],
                scalar1=self.cq[:, 3:4], scalar2=MAGIC,
                op0=ALU.mult, op1=ALU.add),
                waits=[(s_dve, w_sx), (s_dve, w_q2p)])
            self.bank[4] = (s_dve, w_q1a)
            self.bank[5] = (s_dve, w_q1)
            w_q2 = self.vop(lambda e: e.tensor_scalar(
                out=arb[:, OFF_B2S0:OFF_B2S0 + 1024], in0=arf[:, 2048:3072],
                scalar1=-MAGIC, scalar2=None, op0=ALU.add),
                waits=[(s_dve, w_q1), (s_dve, w_q1a), (s_pe, w_xq_done)])
            w_q2p = w_q2
            wtp2 = None
            for ft in range(NDT):
                wtp2 = self.mm(lambda e, ft=ft: e.transpose(
                    self.ptb[:, 128 * ft:128 * (ft + 1)],
                    arb[:, OFF_B2S0 + 128 * ft:OFF_B2S0 + 128 * (ft + 1)],
                    self.idb[:]),
                    waits=[(s_dve, w_q2), (s_dve, w_m)], fin=(ft == NDT - 1))
            w_cp = 0
            for ft in range(NDT):
                w_cp = self.vop(lambda e, tt=tt, ft=ft: e.tensor_copy(
                    xqA(ft, 128 * tt, 128),
                    self.ptb[:, 128 * ft:128 * (ft + 1)]),
                    waits=[(s_pe, wtp2)]
                    + (prev["gp_region"] if tt == 0 and ft == 0 else []))
            self.ptb_free = (s_dve, w_cp)
            w_xq_done = wtp2
            w_xq_tt[tt] = w_cp
            w_cg_tt[tt] = self.vop(lambda e, tt=tt: e.tensor_scalar(
                out=self.colg[:, tt:tt + 1], in0=self.ixcol[:, tt:tt + 1],
                scalar1=self.swcol[:, 3:4], scalar2=None, op0=ALU.mult),
                waits=[(s_dve, w_ix), (s_dve, self.w_swc)])
            w_cg_tt[tt] = self.vop(lambda e, tt=tt: e.tensor_scalar(
                out=self.colu[:, tt:tt + 1], in0=self.ixcol[:, tt:tt + 1],
                scalar1=self.swcol[:, 4:5], scalar2=None, op0=ALU.mult),
                waits=[(s_dve, w_ix), (s_dve, self.w_swc)])
        w_xq_all = self.s_dve.n

        if self.dump == "ffn1":
            d = self.douts
            wall = [(s_pe, self.s_pe.n), (s_act, self.s_act.n),
                    (s_dve, self.s_dve.n)]
            for ft in range(NDT):
                b.op("sync", lambda e, ft=ft: e.dma_start(
                    out=d["d_rs"][128 * ft:128 * (ft + 1), :],
                    in_=self.rs[ft][:]), waits=wall, inc=self.s_st, inc_n=16)
                b.op("sync", lambda e, ft=ft: e.dma_start(
                    out=d["d_xq"][128 * ft:128 * (ft + 1), :],
                    in_=xqA(ft, 0, T)), waits=wall, inc=self.s_st, inc_n=16)
            b.op("sync", lambda e, NTT=NTT: e.dma_start(
                out=d["d_m2"][:, 0:NTT], in_=self.ixcol[:, 0:NTT]),
                 waits=wall, inc=self.s_st, inc_n=16)
            return



        w_lq = [0, 0]
        w_zs = [(self.s_zst[0], 0), (self.s_zst[1], 0)]
        gb = {0: 0, 1: 1, 2: 2, 3: 3}
        w_m2 = {}
        wq_readers = [(s_pe, 0), (s_pe, 0)]
        for oc in range(8):
            w_lq[0] = b.op("sync", lambda e, oc=oc: e.dma_start(
                out=arb[:, WQG:WQG + 4096],
                in_=self.sqg.rearrange("(kt p) o -> p kt o", p=128)
                [:, :, 512 * oc:512 * (oc + 1)]),
                waits=[wq_readers[0], self.cc_done]
                + (prev["hn_region"] if oc == 0 else []),
                inc=self.s_lqg, inc_n=16)
            w_lq[1] = b.op("sync", lambda e, oc=oc: e.dma_start(
                out=arb[:, WQU:WQU + 4096],
                in_=self.squ.rearrange("(kt p) o -> p kt o", p=128)
                [:, :, 512 * oc:512 * (oc + 1)]),
                waits=[wq_readers[1], self.cc_done]
                + (prev["hn_region"] if oc == 0 else []),
                inc=self.s_lqu, inc_n=16)
            for tt in range(NTT):
                bg = self.pbk[2 * (tt % 2)]
                bu = self.pbk[2 * (tt % 2) + 1]
                lastg = lastu = None
                for kt in range(NDT):
                    w_ = []
                    if kt == 0:
                        w_ = [(self.s_lqg, w_lq[0]), (s_dve, w_xq_tt[tt]),
                              self.bank[2 * (tt % 2)]]
                    lastg = self.mm(lambda e, kt=kt, tt=tt, bg=bg: e.matmul(
                        bg[:, :], xqA(kt, 128 * tt, 128),
                        arb[:, WQG + 512 * kt:WQG + 512 * (kt + 1)],
                        start=(kt == 0), stop=(kt == NDT - 1)),
                        waits=w_, fin=(kt == NDT - 1))
                for kt in range(NDT):
                    w_ = []
                    if kt == 0:
                        w_ = [(self.s_lqu, w_lq[1]), self.bank[2 * (tt % 2) + 1]]
                    lastu = self.mm(lambda e, kt=kt, tt=tt, bu=bu: e.matmul(
                        bu[:, :], xqA(kt, 128 * tt, 128),
                        arb[:, WQU + 512 * kt:WQU + 512 * (kt + 1)],
                        start=(kt == 0), stop=(kt == NDT - 1)),
                        waits=w_, fin=(kt == NDT - 1))
                zi = tt % 2
                wgs = self.aop(lambda e, tt=tt, bg=bg: e.activation(
                    out=arf[:, 0:512], in_=bg[:, :], func=AF.Silu,
                    scale=self.colg[:, tt:tt + 1]),
                    waits=[(s_pe, lastg), (s_dve, w_cg_tt[tt]),
                           (s_dve, self.gs_free)])
                wut = self.aop(lambda e, tt=tt, bu=bu: e.activation(
                    out=arf[:, 512:1024], in_=bu[:, :], func=AF.Identity,
                    scale=self.colu[:, tt:tt + 1]),
                    waits=[(s_pe, lastu), (s_dve, w_cg_tt[tt]),
                           (s_dve, self.gs_free)])
                self.bank[2 * (tt % 2)] = (s_act, wgs)
                self.bank[2 * (tt % 2) + 1] = (s_act, wut)
                wz = self.vop(lambda e, zi=zi: e.tensor_tensor(
                    out=arb[:, OFF_B2S0 + 512 * zi:OFF_B2S0 + 512 * (zi + 1)],
                    in0=arf[:, 0:512], in1=arf[:, 512:1024], op=ALU.mult),
                    waits=[(s_act, wgs), (s_act, wut), w_zs[zi]]
                    + (prev["b2_region"] if oc == 0 and tt < 2 else []))
                self.gs_free = wz
                wrm = self.vop(lambda e, zi=zi: e.tensor_reduce(
                    self.cq[:, 4:5],
                    arb[:, OFF_B2S0 + 512 * zi:OFF_B2S0 + 512 * (zi + 1)],
                    AX.X, ALU.max, apply_absolute_value=True),
                    waits=[(s_dve, wz)])
                if oc == 0:
                    w_m2[tt] = self.vop(lambda e, tt=tt: e.tensor_copy(
                        self.m2run[:, tt:tt + 1], self.cq[:, 4:5]),
                        waits=[(s_dve, wrm)])
                else:
                    w_m2[tt] = self.vop(lambda e, tt=tt: e.tensor_tensor(
                        out=self.m2run[:, tt:tt + 1],
                        in0=self.m2run[:, tt:tt + 1],
                        in1=self.cq[:, 4:5], op=ALU.max),
                        waits=[(s_dve, wrm), (s_dve, w_m2[tt])])
                w_zst = b.op("sync", lambda e, oc=oc, tt=tt, zi=zi: e.dma_start(
                    out=self.zhbm[128 * tt:128 * (tt + 1),
                                  512 * oc:512 * (oc + 1)],
                    in_=arb[:, OFF_B2S0 + 512 * zi:
                            OFF_B2S0 + 512 * (zi + 1)]),
                    waits=[(s_dve, wrm)], inc=self.s_zst[zi], inc_n=16)
                w_zs[zi] = (self.s_zst[zi], self.s_zst[zi].n)
            wq_readers[0] = (s_pe, lastg)
            wq_readers[1] = (s_pe, lastu)

        w_sd_readers = (s_pe, 0)
        zq_readers = (s_pe, 0)
        dn = {0: 0, 1: 1, 2: 2, 3: 3}
        w_sd_readers = (s_pe, self.s_pe.n)
        ZLDB = 32960
        self.zldfree = {0: wq_readers[0], 1: wq_readers[0]}
        for (dc0, dcn) in chunks_of(T, 512):
            ntl = dcn // 128
            w_sd = {}
            w_sd[0] = b.op("gpsimd", lambda e: e.dma_start(
                out=arb[:, SD0:SD0 + 16384],
                in_=self.sqd.rearrange("(kt p) eo -> p kt eo", p=128)
                [:, :, 0:512]),
                waits=[w_sd_readers, self.cc_done],
                inc=self.s_lsd, inc_n=16)
            w_zq_tl = {}
            w_zq_pi = {}
            dn_free = {}

            def quant_tl(tl):
                tt = dc0 // 128 + tl
                w_c = self.vop(lambda e, tt=tt: e.tensor_scalar(
                    out=self.cq[:, 1:2], in0=self.m2run[:, tt:tt + 1],
                    scalar1=1e-5, scalar2=None, op0=ALU.max),
                    waits=[(s_dve, w_m2[tt]), (s_dve, self.cq_free)])
                w_r = self.vop(lambda e: e.reciprocal(
                    self.cq[:, 2:3], self.cq[:, 1:2]), waits=[(s_dve, w_c)])
                w_s2 = self.vop(lambda e: e.tensor_scalar(
                    out=self.cq[:, 3:4], in0=self.cq[:, 2:3], scalar1=127.0,
                    scalar2=None, op0=ALU.mult), waits=[(s_dve, w_r)])
                w_cd = self.vop(lambda e: e.tensor_scalar(
                    out=self.cq[:, 5:6], in0=self.cq[:, 1:2],
                    scalar1=1.0 / 127.0, scalar2=None, op0=ALU.mult),
                    waits=[(s_dve, w_c)])
                w_cd = self.vop(lambda e, tt=tt: e.tensor_scalar(
                    out=self.colD[:, tt:tt + 1], in0=self.cq[:, 5:6],
                    scalar1=self.swcol[:, 5:6], scalar2=None, op0=ALU.mult),
                    waits=[(s_dve, w_cd)])
                self.cq_free = w_cd
                w_cp = 0
                for pi in range(4):
                    zb_ = pi % 2
                    w_zl = b.op("sync", lambda e, tt=tt, pi=pi, zb_=zb_:
                                e.dma_start(
                                    out=arb[:, ZLDB + 1024 * zb_:
                                            ZLDB + 1024 * (zb_ + 1)],
                                    in_=self.zhbm[128 * tt:128 * (tt + 1),
                                                  1024 * pi:1024 * (pi + 1)]),
                                waits=[self.zldfree[zb_],
                                       (self.s_zst[0], self.s_zst[0].n),
                                       (self.s_zst[1], self.s_zst[1].n)],
                                inc=self.s_lf[zb_], inc_n=16)
                    w_q1 = self.vop(lambda e, zb_=zb_: e.tensor_scalar(
                        out=arf[:, 2048:3072],
                        in0=arb[:, ZLDB + 1024 * zb_:ZLDB + 1024 * (zb_ + 1)],
                        scalar1=self.cq[:, 3:4], scalar2=MAGIC,
                        op0=ALU.mult, op1=ALU.add),
                        waits=[(self.s_lf[zb_], w_zl), (s_dve, w_s2),
                               (s_dve, self.tqf_free)])
                    self.zldfree[zb_] = (s_dve, w_q1)
                    w_q2 = self.vop(lambda e, pi=pi: e.tensor_scalar(
                        out=arb[:, ZTM + 1024 * pi:ZTM + 1024 * (pi + 1)],
                        in0=arf[:, 2048:3072],
                        scalar1=-MAGIC, scalar2=None, op0=ALU.add),
                        waits=[(s_dve, w_q1), (s_pe, self.ztm_free)])
                    self.tqf_free = w_q2
                    wtp = None
                    for fb in range(8):
                        wtp = self.mm(lambda e, pi=pi, fb=fb: e.transpose(
                            self.ptb[:, 128 * fb:128 * (fb + 1)],
                            arb[:, ZTM + 1024 * pi + 128 * fb:
                                ZTM + 1024 * pi + 128 * (fb + 1)],
                            self.idb[:]),
                            waits=[(s_dve, w_q2), self.ptb_free],
                            fin=(fb == 7))
                    self.ztm_free = wtp
                    w_cp = 0
                    for fb in range(8):
                        kt = pi * 8 + fb
                        w_cp = self.aop(lambda e, kt=kt, fb=fb, tl=tl:
                                        e.activation(
                                            out=arb[:, ZQ0 + 512 * kt + 128 * tl:
                                                ZQ0 + 512 * kt + 128 * (tl + 1)],
                                            in_=self.ptb[:, 128 * fb:
                                                         128 * (fb + 1)],
                                            func=AF.Identity),
                                        waits=[(s_pe, wtp), zq_readers]
                                        + (prev["vp_region"]
                                           if dc0 == 0 and tl == 0 and pi == 0
                                           and fb == 0 else []))
                    self.ptb_free = (s_act, w_cp)
                    w_zq_pi[(tl, pi)] = w_cp
                w_zq_tl[tl] = w_cp

            def down_tl(ec, tl):
                tt = dc0 // 128 + tl
                key = (ec + tl) % 4
                pbank = self.pbk[dn[key]]
                last = None
                for kt in range(NFT):
                    w_ = []
                    if kt == 0:
                        w_ = [(self.s_lsd, w_sd[ec]),
                              (s_act, w_zq_pi[(tl, 0)]),
                              self.bank[dn[key]]]
                    elif kt % 8 == 0:
                        w_ = [(s_act, w_zq_pi[(tl, kt // 8)])]
                    last = self.mm(lambda e, kt=kt, tl=tl, pbank=pbank:
                                   e.matmul(
                                       pbank[:, :],
                                       arb[:, ZQ0 + 512 * kt + 128 * tl:
                                           ZQ0 + 512 * kt + 128 * (tl + 1)],
                                       arb[:, SD0 + 512 * kt:
                                           SD0 + 512 * (kt + 1)],
                                       start=(kt == 0), stop=(kt == NFT - 1)),
                                   waits=w_, fin=(kt == NFT - 1))
                dn_free[ec] = last
                wdt = self.aop(lambda e, tt=tt, pbank=pbank: e.activation(
                    out=arf[:, 3072:3584], in_=pbank[:, :], func=AF.Identity,
                    scale=self.colD[:, tt:tt + 1]),
                    waits=[(s_pe, last), (s_dve, self.cq_free),
                           (s_pe, self.dtm_free)])
                self.bank[dn[key]] = (s_act, wdt)
                wtps = {}
                for eb in range(4):
                    wtps[eb] = self.mm(lambda e, eb=eb: e.transpose(
                        self.pbk[5][:, 128 * eb:128 * (eb + 1)],
                        arf[:, 3072 + 128 * eb:3072 + 128 * (eb + 1)],
                        self.idf[:]),
                        waits=[(s_act, wdt)]
                        + ([self.bank[5]] if eb == 0 else []), fin=True)
                wtp = wtps[3]
                for eb in range(4):
                    et = ec * 4 + eb
                    w_add = self.vop(lambda e, et=et, tt=tt, eb=eb:
                                     e.tensor_tensor(
                                         out=self.rs[et][:, 128 * tt:
                                                         128 * (tt + 1)],
                                         in0=self.rs[et][:, 128 * tt:
                                                         128 * (tt + 1)],
                                         in1=self.pbk[5][:, 128 * eb:
                                                         128 * (eb + 1)],
                                         op=ALU.add),
                                     waits=[(s_pe, wtps[3])])
                self.bank[5] = (s_dve, w_add)
                self.dtm_free = wtp
                self.w_rs2 = w_add

            quant_tl(0)
            for tl in range(ntl):
                if tl + 1 < ntl:
                    quant_tl(tl + 1)
                down_tl(0, tl)
            w_sd[1] = b.op("gpsimd", lambda e: e.dma_start(
                out=arb[:, SD0:SD0 + 16384],
                in_=self.sqd.rearrange("(kt p) eo -> p kt eo", p=128)
                [:, :, 512:1024]),
                waits=[(s_pe, dn_free[0])], inc=self.s_lsd, inc_n=16)
            for tl in range(ntl):
                down_tl(1, tl)
            w_sd_readers = (s_pe, dn_free[1])
            zq_readers = (s_pe, self.s_pe.n)
            w_ln2 = self.layernorm_chunks(0, [(dc0, dcn)],
                                          wait_rs=(s_dve, self.w_rs2))
            for ft in range(NDT):
                b.op("sync", lambda e, ft=ft, dc0=dc0, dcn=dcn: e.dma_start(
                    out=self.o_out[128 * ft:128 * (ft + 1), dc0:dc0 + dcn],
                    in_=self.rs[ft][:, dc0:dc0 + dcn]),
                    waits=[(s_dve, w_ln2)], inc=self.s_out, inc_n=16)

    def dump_pre(self):
        b, arb = self.b, self.arb
        w_all = [(self.s_pe, self.s_pe.n), (self.s_act, self.s_act.n),
                 (self.s_dve, self.s_dve.n),
                 (self.s_sqb[0], self.s_sqb[0].n),
                 (self.s_sqb[1], self.s_sqb[1].n)]
        d = self.douts
        for m in range(NIT):
            b.op("sync", lambda e, m=m: e.dma_start(
                out=d["d_whp"][128 * m:128 * (m + 1), :],
                in_=arb[:, OFF_WHP(m):OFF_WHP(m) + 1024]),
                waits=w_all, inc=self.s_st, inc_n=16)
        b.op("sync", lambda e: e.dma_start(out=d["d_g16"], in_=self.g16p[:]),
             waits=w_all, inc=self.s_st, inc_n=16)
        b.op("sync", lambda e: e.dma_start(out=d["d_ybr"], in_=self.ybr[:]),
             waits=w_all, inc=self.s_st, inc_n=16)
        b.op("sync", lambda e: e.dma_start(out=d["d_hpb"], in_=self.hpbc[:]),
             waits=w_all, inc=self.s_st, inc_n=16)
        b.op("sync", lambda e: e.dma_start(out=d["d_a4"], in_=self.a4[:]),
             waits=w_all, inc=self.s_st, inc_n=16)
        b.op("sync", lambda e: e.dma_start(out=d["d_scl"], in_=self.scl[:]),
             waits=w_all, inc=self.s_st, inc_n=16)
        for kt in range(NDT):
            b.op("sync", lambda e, kt=kt: e.dma_start(
                out=d["d_sqg"][128 * kt:128 * (kt + 1), :],
                in_=self.sqg[128 * kt:128 * (kt + 1), :]),
                waits=w_all, inc=self.s_st, inc_n=16)

    def dump_attn(self):
        b = self.b
        w_all = [(self.s_pe, self.s_pe.n), (self.s_act, self.s_act.n),
                 (self.s_dve, self.s_dve.n)]
        for ft in range(NDT):
            b.op("sync", lambda e, ft=ft: e.dma_start(
                out=self.douts["d_rs"][128 * ft:128 * (ft + 1), :],
                in_=self.rs[ft][:]),
                waits=w_all, inc=self.s_st, inc_n=16)

    def finish(self):
        self.b.op("sync", lambda e: e.nop(),
                  waits=[(self.s_st, self.s_st.n),
                         (self.s_sqb[0], self.s_sqb[0].n),
                         (self.s_sqb[1], self.s_sqb[1].n)])
        self.b.op("sync", lambda e: e.nop(),
                  waits=[(self.s_out, self.s_out.n)])



def _colf(v, n):
    return np.ascontiguousarray(np.asarray(v, np.float32).reshape(n, 128).T)


def _colb(v, n):
    return np.ascontiguousarray(
        np.asarray(v, np.float32).reshape(n, 128).T.astype(ml_dtypes.bfloat16))


def prep_shared(i):
    f32 = np.float32
    g = lambda k: np.asarray(i[k], f32)
    sh = {}
    inwT = g("L0_attn_in_w").T
    sh["in_sw"] = np.ascontiguousarray(
        inwT.reshape(8, 128, 16, 256).transpose(1, 2, 0, 3).reshape(128, 32768))
    pw = g("L0_attn_pw_w")
    sh["pw_sw"] = np.ascontiguousarray(
        pw.reshape(16, 128, 2048).transpose(1, 2, 0).reshape(128, 16 * 2048))
    sh["hb_wT"] = np.ascontiguousarray(g("L0_attn_hb_w").T)
    fwT = g("L0_attn_fuse_w").T
    sh["fuse_sw"] = np.ascontiguousarray(
        fwT.reshape(16, 128, 8, 256).transpose(1, 2, 0, 3).reshape(128, 32768))
    owT = g("L0_attn_out_w").T
    sh["out_sw"] = np.ascontiguousarray(
        owT.reshape(16, 128, 8, 128).transpose(1, 2, 0, 3).reshape(128, 16384))
    sh["ha_wT"] = np.ascontiguousarray(g("L0_attn_ha_w").T)
    sh["BmT"] = np.ascontiguousarray(g("L0_attn_Bm").T)
    A = g("L0_attn_A")
    sh["A_t"] = np.ascontiguousarray(
        A.reshape(16, 128, 16).transpose(1, 0, 2).reshape(128, 256))
    dk = g("L0_attn_dwa_k")[:, 0, :]
    sh["dwa_t"] = np.ascontiguousarray(
        dk.reshape(16, 128, 4).transpose(1, 0, 2).reshape(128, 64))
    sk = g("L0_attn_sym_k")[:, 0, :]
    sh["sym_t"] = np.ascontiguousarray(
        sk.reshape(16, 128, 9).transpose(1, 0, 2).reshape(128, 144))
    sh["wgsh"] = np.ascontiguousarray(g("L0_ffn_w_gate").T)
    sh["wush"] = np.ascontiguousarray(g("L0_ffn_w_up").T)
    sh["wdsh"] = np.ascontiguousarray(g("L0_ffn_w_down").T)
    sh["consts"] = np.tile(np.array(
        [[1e-6, 1e-5, MAGIC, 1.0, 1.0 / 127.0, 1.0 / D, 127.0, 2.0 ** -22]],
        dtype=f32), (128, 1))
    sh["norm_w_c"] = _colf(i["L0_attn_norm_w"], NDT)
    sh["in_b_c"] = _colf(i["L0_attn_in_b"], 4 * NDT)
    sh["dwa_b_cb"] = _colb(i["L0_attn_dwa_b"], NIT)
    sh["sym_b_c"] = _colf(i["L0_attn_sym_b"], NIT)
    sh["pw_b_cb"] = _colb(i["L0_attn_pw_b"], NIT)
    sh["hb_b_c"] = _colf(i["L0_attn_hb_b"], NDT)
    sh["ha_b_c"] = _colf(i["L0_attn_ha_b"], NDT)
    sh["fuse_b_c"] = _colf(i["L0_attn_fuse_b"], NIT)
    obr = np.zeros((32, D), dtype=ml_dtypes.bfloat16)
    obr[0, :] = np.asarray(i["L0_attn_out_b"], f32).astype(ml_dtypes.bfloat16)
    sh["out_b_r"] = obr
    sh["ones32c"] = np.ones((128, 32), dtype=ml_dtypes.bfloat16)
    sh["ones32cf"] = np.ones((128, 32), dtype=f32)
    o32rf = np.zeros((32, 128), dtype=f32)
    o32rf[0, :] = 1.0
    sh["ones32rf"] = o32rf
    oNb = np.zeros((32, 512), dtype=ml_dtypes.bfloat16)
    oNb[0, :] = 1.0
    sh["onesNb"] = oNb
    sh["identb"] = np.eye(128, dtype=ml_dtypes.bfloat16)
    sh["identf"] = np.eye(128, dtype=f32)
    return sh


def kernel(**inputs):
    x = np.asarray(inputs["L0_attn_x"], np.float32)
    sh = prep_shared(inputs)
    p = Prog(T=2048, dump=None, ncores=8).build()
    wgsh, wush, wdsh = sh.pop("wgsh"), sh.pop("wush"), sh.pop("wdsh")
    in_maps = []
    for c in range(8):
        m = dict(sh)
        m["xT"] = np.ascontiguousarray(x[c].T)
        m["wgsh"] = np.ascontiguousarray(wgsh[128 * c:128 * (c + 1)])
        m["wush"] = np.ascontiguousarray(wush[128 * c:128 * (c + 1)])
        m["wdsh"] = np.ascontiguousarray(wdsh[512 * c:512 * (c + 1)])
        in_maps.append(m)
    res = run_bass_kernel_spmd(p.nc, in_maps, core_ids=list(range(8)))
    out = np.stack([np.asarray(r["o_out"]).T for r in res.results], axis=0)
    return out.astype(np.float32)
